# version 08 (final)
# baseline (speedup 1.0000x reference)
import numpy as np
import ml_dtypes
import contextlib

import concourse.bass as bass
import concourse.mybir as mybir
from concourse.bass_utils import run_bass_kernel_spmd

F32 = mybir.dt.float32
BF16 = mybir.dt.bfloat16
Alu = mybir.AluOpType
AF = mybir.ActivationFunctionType

D = 1024
H = 8
NN = 32
KD = H * NN
VD = 512
HV = VD // H
DC = 4
BL = 2
NCORES = 8
EPS_LN = 1e-5
EPS_NORM = 1e-6
EPS_RMS = 1e-5
CH0 = 8


def _r(x):
    return np.ascontiguousarray(x)


def qk_perm():
    p = np.zeros(KD, dtype=np.int64)
    for h in range(H):
        for nhi in range(4):
            for nlo in range(8):
                p[nhi * 64 + h * 8 + nlo] = h * 32 + nhi * 8 + nlo
    return p


class Prog:
    def __init__(self, T=2048, Tc=32, TTILE=512, stop_phase="full"):
        self.T = T
        self.Tc = Tc
        self.TTILE = TTILE
        self.P1TILE = min(256, TTILE)
        self.BLT = BL * T
        self.stop_phase = stop_phase
        assert T % Tc == 0 and T % TTILE == 0 and T % self.P1TILE == 0

    def build(self):
        T, Tc, TTILE, BLT = self.T, self.Tc, self.TTILE, self.BLT
        P1T = self.P1TILE
        NCH = T // Tc
        NTT = T // TTILE
        NT1 = T // P1T

        nc = bass.Bass("TRN2", target_bir_lowering=False,
                       detect_race_conditions=False)
        self.nc = nc

        din = {}

        def dram_in(name, shape, dt=F32):
            din[name] = nc.dram_tensor(name, shape, dt, kind="ExternalInput").ap()
            return din[name]

        xT = dram_in("xT", [D, BLT])
        wqT = dram_in("wqT", [128, 8 * KD], BF16)
        wkT = dram_in("wkT", [128, 8 * KD], BF16)
        wvT = dram_in("wvT", [128, 8 * VD], BF16)
        waT = dram_in("waT", [128, 8 * H], BF16)
        wgT = dram_in("wgT", [128, 8 * VD], BF16)
        woT = dram_in("woT", [128, 4 * D], BF16)
        qcw = dram_in("qcw", [128, 2 * DC])
        kcw = dram_in("kcw", [128, 2 * DC])
        vcw = dram_in("vcw", [128, 4 * DC])
        alog = dram_in("alog", [H, 1])
        dtb = dram_in("dtb", [H, 1])
        onorm = dram_in("onorm", [128, 1])
        cw1 = dram_in("cw1", [128, 8 * D], BF16)
        cw2 = dram_in("cw2", [128, 8 * 512], BF16)
        cw3 = dram_in("cw3", [128, 4 * 256], BF16)
        cw4 = dram_in("cw4", [128, 2], BF16)
        cb1 = dram_in("cb1", [128, 8])
        cb2 = dram_in("cb2", [128, 4])
        cb3 = dram_in("cb3", [128, 2])
        cb4 = dram_in("cb4", [1, 1])
        gw1 = dram_in("gw1", [128, 8 * 512])
        gw2 = dram_in("gw2", [128, 4 * 256])
        gw3 = dram_in("gw3", [128, 2])
        gb1 = dram_in("gb1", [128, 4])
        gb2 = dram_in("gb2", [128, 2])
        gb3 = dram_in("gb3", [1, 1])
        arng = dram_in("arng", [1, 1])
        p_ones128 = dram_in("p_ones128", [128, 1], BF16)
        p_ones1 = dram_in("p_ones1", [1, 128])
        p_hsum = dram_in("p_hsum", [128, H], BF16)
        p_rep8h = dram_in("p_rep8h", [H, 128])
        p_hv8 = dram_in("p_hv8", [128, 4 * H], BF16)
        p_repfm = dram_in("p_repfm", [H, 4 * 128])
        p_rep8d = dram_in("p_rep8d", [H, BL * 128])

        yT = nc.dram_tensor("yT", [D, BLT], F32, kind="ExternalOutput").ap()

        vz_dram = nc.dram_tensor("vz_dram", [16, HV, T], F32, kind="Internal").ap()
        sq_dram = nc.dram_tensor("sq_dram", [16, HV, T], F32, kind="Internal").ap()
        gate_dram = nc.dram_tensor("gate_dram", [VD, BLT], F32, kind="Internal").ap()
        xa_dram = nc.dram_tensor("xa_dram", [D, BLT], F32, kind="Internal").ap()
        ln2_dram = nc.dram_tensor("ln2_dram", [D, BLT], F32, kind="Internal").ap()

        st = contextlib.ExitStack()

        def sb(name, shape, dt=F32):
            return st.enter_context(nc.sbuf_tensor(name, shape, dt))

        def sem(name):
            return st.enter_context(nc.semaphore(name))

        kn_bf = [sb(f"kn_bf{i}", [128, BLT + 1], BF16) for i in range(2)]
        qn_bf = [sb(f"qn_bf{i}", [128, BLT + 1], BF16) for i in range(2)]
        d_scan = sb("d_scan", [128, T])

        ones128 = sb("ones128", [128, 1], BF16)
        ones1 = sb("ones1", [1, 128])
        hsum = sb("hsum", [128, H], BF16)
        rep8h = sb("rep8h", [H, 128])
        hv8s = sb("hv8s", [128, 4 * H], BF16)
        repfms = sb("repfms", [H, 4 * 128])
        rep8ds = sb("rep8ds", [H, BL * 128])
        onorm_pp = sb("onorm_pp", [128, 1])
        dtb_pp = sb("dtb_pp", [H, 1])
        negexpA = sb("negexpA", [H, 1])
        cb1_sb = sb("cb1_sb", [128, 8])
        cb2_sb = sb("cb2_sb", [128, 4])
        cb3_sb = sb("cb3_sb", [128, 2])
        cb4_sb = sb("cb4_sb", [1, 1])
        gb1_sb = sb("gb1_sb", [128, 4])
        gb2_sb = sb("gb2_sb", [128, 2])
        gb3_sb = sb("gb3_sb", [1, 1])
        ar_sb = sb("ar_sb", [1, 1])
        qcw_sb = sb("qcw_sb", [128, 2, DC])
        kcw_sb = sb("kcw_sb", [128, 2, DC])
        vcw_sb = sb("vcw_sb", [128, 4, DC])
        gf_A = sb("gf_A", [1, BL])
        gf_B = sb("gf_B", [1, 1])
        epsln_1 = sb("epsln_1", [1, 1])
        epsrms_h = sb("epsrms_h", [H, 1])
        rms_sc_h = sb("rms_sc_h", [H, 1])

        s_v = sem("s_v")
        s_s = sem("s_s")
        s_t = sem("s_t")
        s_g = sem("s_g")
        s_ld = sem("s_ld")
        s_x = sem("s_x")
        s_bld = sem("s_bld")
        s_vz = sem("s_vz")
        s_sqo = sem("s_sqo")
        s_io = sem("s_io")
        s_gld = sem("s_gld")

        c = dict(v=0, s=0, t=0, g=0, ld=0, x=0, bld=0, vz=0, sqo=0, io=0,
                 gld=0)
        esem = dict(v=s_v, s=s_s, t=s_t, g=s_g)

        SY = nc.sync

        class _Eng:
            def __init__(self, eng, key, self_wait):
                self._eng = eng
                self._key = key
                self._sw = self_wait
                self._lastw = 0

            def __getattr__(self, name):
                real = getattr(self._eng, name)
                if name in ("wait_ge", "dma_start") or not callable(real):
                    return real
                me = self._key

                def call(*a, **kw):
                    if self._sw and c[me] > self._lastw:
                        self._eng.wait_ge(esem[me], c[me])
                        self._lastw = c[me]
                    r = real(*a, **kw)
                    r.then_inc(esem[me], 1)
                    c[me] += 1
                    return r
                return call

        V = _Eng(nc.vector, "v", True)
        S = _Eng(nc.scalar, "s", True)
        G = _Eng(nc.gpsimd, "g", True)
        TEN = _Eng(nc.tensor, "t", False)

        def te_w():
            nc.tensor.wait_ge(s_t, c["t"])

        blk_box = {}

        def barrier():
            blk_box["cm"].__exit__(None, None, None)
            cm = nc.Block()
            blk_box["cm"] = cm
            cm.__enter__()

        stop_phase = self.stop_phase

        def phase_on(name):
            order = ["p0", "p1", "p2", "p3", "p4a", "p4b", "full"]
            want = stop_phase if stop_phase != "full" else "p4b"
            return order.index(name) <= order.index(want)

        blk_box["cm"] = nc.Block()
        blk_box["cm"].__enter__()
        if True:
            for dst, src in ((onorm_pp, onorm), (dtb_pp, dtb),
                             (cb1_sb, cb1), (cb2_sb, cb2), (cb3_sb, cb3),
                             (cb4_sb, cb4), (gb1_sb, gb1), (gb2_sb, gb2),
                             (gb3_sb, gb3), (ar_sb, arng), (negexpA, alog)):
                SY.dma_start(out=dst.ap(), in_=src).then_inc(s_ld, 16)
                c["ld"] += 16
            for dst, srcd in ((qcw_sb, qcw), (kcw_sb, kcw), (vcw_sb, vcw)):
                SY.dma_start(out=dst.ap().rearrange("p a b -> p (a b)"),
                             in_=srcd).then_inc(s_ld, 16)
                c["ld"] += 16
            for dst, srcd in ((ones128, p_ones128), (ones1, p_ones1),
                              (hsum, p_hsum), (rep8h, p_rep8h),
                              (hv8s, p_hv8), (repfms, p_repfm),
                              (rep8ds, p_rep8d)):
                SY.dma_start(out=dst.ap(), in_=srcd).then_inc(s_ld, 16)
                c["ld"] += 16
            for kk in range(2):
                V.memset(kn_bf[kk].ap()[:, BLT:BLT + 1], 0.0)
                V.memset(qn_bf[kk].ap()[:, BLT:BLT + 1], 0.0)
            V.memset(epsln_1.ap(), EPS_LN)
            V.memset(epsrms_h.ap(), EPS_RMS)
            V.memset(rms_sc_h.ap(), 1.0 / HV)
            nc.scalar.wait_ge(s_ld, c["ld"])
            S.activation(negexpA.ap(), negexpA.ap(), AF.Exp)
            nc.vector.wait_ge(s_s, c["s"])
            V.tensor_scalar_mul(negexpA.ap(), negexpA.ap(), -1.0)
            barrier()

            if phase_on("p1"):
                with contextlib.ExitStack() as p1:
                    def sb1(name, shape, dt=F32):
                        return p1.enter_context(nc.sbuf_tensor(name, shape, dt))

                    def ps1(name, shape, dt=F32):
                        return p1.enter_context(nc.psum_tensor(name, shape, dt))

                    wq_sb = sb1("wq_sb", [128, KD, 8], BF16)
                    wk_sb = sb1("wk_sb", [128, KD, 8], BF16)
                    wv_sb = sb1("wv_sb", [128, VD, 8], BF16)
                    wa_sb = sb1("wa_sb", [128, H, 8], BF16)
                    wg_sb = sb1("wg_sb", [128, VD, 8], BF16)
                    for wsb, wdr in ((wq_sb, wqT), (wk_sb, wkT), (wv_sb, wvT),
                                     (wa_sb, waT), (wg_sb, wgT)):
                        G.dma_start(out=wsb.ap(),
                                    in_=wdr.rearrange("(c p) m -> p c m", p=128)).then_inc(s_gld, 16)
                        c["gld"] += 16
                    nc.tensor.wait_ge(s_gld, c["gld"])

                    d_dec = sb1("d_dec", [H, BLT])
                    xt = sb1("xt", [128, 8, P1T])
                    xt_bf = sb1("xt_bf", [128, 8, P1T], BF16)
                    x2b8 = sb1("x2b8", [128, 8, P1T], BF16)
                    lnt = sb1("lnt", [128, 8, P1T])
                    lnt_bf = sb1("lnt_bf", [128, 8, P1T], BF16)
                    mu = sb1("mu", [1, P1T])
                    musq = sb1("musq", [1, P1T])
                    var = sb1("var", [1, P1T])
                    rstd = sb1("rstd", [1, P1T])
                    murstd = sb1("murstd", [1, P1T])
                    sx_ps = ps1("sx_ps", [1, P1T])
                    sx2_ps = ps1("sx2_ps", [1, P1T])
                    rep_ps = ps1("rep_ps", [128, 2 * P1T])
                    pj_ps = [ps1(f"pj_ps{i}", [128, P1T]) for i in range(2)]
                    n_ps = ps1("n_ps", [H, P1T])
                    krep_ps = ps1("krep_ps", [128, P1T])

                    CW = CH0 + P1T
                    convq = [sb1(f"convq{i}", [128, CW]) for i in range(2)]
                    convk = [sb1(f"convk{i}", [128, CW]) for i in range(2)]
                    convv = [sb1(f"convv{i}", [128, CW]) for i in range(4)]
                    silq = [sb1(f"silq{i}", [128, P1T]) for i in range(2)]
                    silk = [sb1(f"silk{i}", [128, P1T]) for i in range(2)]
                    silv = sb1("silv", [128, P1T])
                    sqb = [sb1(f"sqb{i}", [128, P1T], BF16) for i in range(2)]
                    tmp1 = sb1("tmp1", [128, P1T])
                    nrm = sb1("nrm", [H, P1T])
                    gsig = sb1("gsig", [128, P1T])
                    aevac = sb1("aevac", [H, P1T])

                    def proj_mm(psum_ap, w_sb, col0, Mcols, src_bf):
                        te_w()
                        nc.tensor.wait_ge(s_v, c["v"])
                        nc.tensor.wait_ge(s_s, c["s"])
                        for dc in range(8):
                            TEN.matmul(psum_ap[0:Mcols, :],
                                       w_sb.ap()[:, col0:col0 + Mcols, dc],
                                       src_bf.ap()[:, dc, :],
                                       start=(dc == 0), stop=(dc == 7))

                    def conv_silu(cin, w_pp_sb, wi, sil_out):
                        nc.vector.wait_ge(s_s, c["s"])
                        V.tensor_scalar_mul(tmp1.ap(), cin.ap()[:, CH0:CH0 + P1T],
                                            w_pp_sb.ap()[:, wi, DC - 1:DC])
                        for j in range(DC - 1):
                            V.scalar_tensor_tensor(
                                tmp1.ap(),
                                cin.ap()[:, CH0 - DC + 1 + j:CH0 - DC + 1 + j + P1T],
                                w_pp_sb.ap()[:, wi, j:j + 1],
                                tmp1.ap(), Alu.mult, Alu.add)
                        V.tensor_copy(cin.ap()[:, CH0 - DC + 1:CH0],
                                      cin.ap()[:, P1T + CH0 - DC + 1:P1T + CH0])
                        nc.scalar.wait_ge(s_v, c["v"])
                        S.activation(sil_out.ap(), tmp1.ap(), AF.Silu)

                    SY.wait_ge(s_gld, c["gld"])
                    for b in range(BL):
                        for cv in convq + convk + convv:
                            V.memset(cv.ap()[:, CH0 - DC + 1:CH0], 0.0)

                        for tt in range(NT1):
                            t0 = b * T + tt * P1T
                            SY.wait_ge(s_v, c["v"])
                            for dh in range(2):
                                SY.dma_start(
                                    out=xt.ap()[:, 4 * dh:4 * (dh + 1), :],
                                    in_=bass.AP(xT.tensor,
                                                4 * dh * 128 * BLT + t0,
                                                [[BLT, 128], [128 * BLT, 4],
                                                 [1, P1T]]),
                                ).then_inc(s_x, 16)
                                c["x"] += 16
                            nc.vector.wait_ge(s_x, c["x"])
                            nc.vector.wait_ge(s_t, c["t"])
                            V.tensor_copy(xt_bf.ap().rearrange("p a t -> p (a t)"),
                                          xt.ap().rearrange("p a t -> p (a t)"))
                            V.tensor_mul(x2b8.ap().rearrange("p a t -> p (a t)"),
                                         xt.ap().rearrange("p a t -> p (a t)"),
                                         xt.ap().rearrange("p a t -> p (a t)"))

                            te_w()
                            nc.tensor.wait_ge(s_v, c["v"])
                            for dc in range(8):
                                TEN.matmul(sx_ps.ap(), ones128.ap(),
                                           xt_bf.ap()[:, dc, :],
                                           start=(dc == 0), stop=(dc == 7))
                                TEN.matmul(sx2_ps.ap(), ones128.ap(),
                                           x2b8.ap()[:, dc, :],
                                           start=(dc == 0), stop=(dc == 7))
                            nc.vector.wait_ge(s_t, c["t"])
                            V.tensor_scalar_mul(mu.ap(), sx_ps.ap(), 1.0 / D)
                            V.tensor_mul(musq.ap(), mu.ap(), mu.ap())
                            V.scalar_tensor_tensor(var.ap(), sx2_ps.ap(), 1.0 / D,
                                                   musq.ap(), Alu.mult, Alu.subtract)
                            nc.scalar.wait_ge(s_v, c["v"])
                            S.activation(rstd.ap(), var.ap(), AF.Sqrt,
                                         bias=epsln_1.ap())
                            nc.vector.wait_ge(s_s, c["s"])
                            V.reciprocal(rstd.ap(), rstd.ap())
                            V.tensor_mul(murstd.ap(), mu.ap(), rstd.ap())
                            te_w()
                            nc.tensor.wait_ge(s_v, c["v"])
                            TEN.matmul(rep_ps.ap()[:, 0:P1T], ones1.ap(),
                                       rstd.ap(), start=True, stop=True)
                            TEN.matmul(rep_ps.ap()[:, P1T:], ones1.ap(),
                                       murstd.ap(), start=True, stop=True)
                            nc.vector.wait_ge(s_t, c["t"])
                            rrep = rep_ps.ap()[:, 0:P1T].unsqueeze(1).broadcast_to([128, 8, P1T])
                            mrep = rep_ps.ap()[:, P1T:].unsqueeze(1).broadcast_to([128, 8, P1T])
                            V.tensor_mul(lnt.ap(), xt.ap(), rrep)
                            V.tensor_sub(lnt_bf.ap(), lnt.ap(), mrep)

                            for name, wsb_, cws, convs, sils in (
                                    ("q", wq_sb, qcw_sb, convq, silq),
                                    ("k", wk_sb, kcw_sb, convk, silk)):
                                for i in range(2):
                                    pp = pj_ps[i % 2]
                                    proj_mm(pp.ap(), wsb_, i * 128, 128, lnt_bf)
                                    nc.vector.wait_ge(s_t, c["t"])
                                    V.tensor_copy(convs[i].ap()[:, CH0:CH0 + P1T],
                                                  pp.ap())
                                    conv_silu(convs[i], cws, i, sils[i])
                                for i in range(2):
                                    nc.vector.wait_ge(s_s, c["s"])
                                    V.tensor_mul(sqb[i].ap(), sils[i].ap(),
                                                 sils[i].ap())
                                te_w()
                                nc.tensor.wait_ge(s_v, c["v"])
                                nc.tensor.wait_ge(s_s, c["s"])
                                for i in range(2):
                                    TEN.matmul(n_ps.ap(), hsum.ap(), sqb[i].ap(),
                                               start=(i == 0), stop=(i == 1))
                                nc.scalar.wait_ge(s_t, c["t"])
                                S.activation(nrm.ap(), n_ps.ap(), AF.Sqrt)
                                nc.vector.wait_ge(s_s, c["s"])
                                V.tensor_scalar_add(nrm.ap(), nrm.ap(), EPS_NORM)
                                V.reciprocal(nrm.ap(), nrm.ap())
                                te_w()
                                nc.tensor.wait_ge(s_v, c["v"])
                                TEN.matmul(krep_ps.ap(), rep8h.ap(), nrm.ap(),
                                           start=True, stop=True)
                                nc.vector.wait_ge(s_t, c["t"])
                                dstt = qn_bf if name == "q" else kn_bf
                                for i in range(2):
                                    V.tensor_mul(dstt[i].ap()[:, t0:t0 + P1T],
                                                 sils[i].ap(), krep_ps.ap())

                            for i in range(4):
                                pp = pj_ps[i % 2]
                                proj_mm(pp.ap(), wv_sb, i * 128, 128, lnt_bf)
                                nc.vector.wait_ge(s_t, c["t"])
                                V.tensor_copy(convv[i].ap()[:, CH0:CH0 + P1T],
                                              pp.ap())
                                nc.scalar.wait_ge(s_io, c["io"])
                                conv_silu(convv[i], vcw_sb, i, silv)
                                SY.wait_ge(s_s, c["s"])
                                SY.dma_start(
                                    out=bass.AP(vz_dram.tensor,
                                                (b * 8 + 2 * i) * HV * T + tt * P1T,
                                                [[HV * T, 2], [T, HV], [1, P1T]]),
                                    in_=silv.ap(),
                                ).then_inc(s_io, 16)
                                c["io"] += 16

                            for i in range(4):
                                pp = pj_ps[i % 2]
                                proj_mm(pp.ap(), wg_sb, i * 128, 128, lnt_bf)
                                nc.scalar.wait_ge(s_t, c["t"])
                                nc.scalar.wait_ge(s_io, c["io"])
                                S.activation(gsig.ap(), pp.ap(), AF.Sigmoid)
                                SY.wait_ge(s_s, c["s"])
                                SY.dma_start(out=gate_dram[i * 128:(i + 1) * 128,
                                                           t0:t0 + P1T],
                                             in_=gsig.ap()).then_inc(s_io, 16)
                                c["io"] += 16

                            proj_mm(pj_ps[0].ap()[0:H, :], wa_sb, 0, H, lnt_bf)
                            nc.scalar.wait_ge(s_t, c["t"])
                            S.activation(aevac.ap(), pj_ps[0].ap()[0:H, :],
                                         AF.Exp, bias=dtb_pp.ap())
                            S.activation(aevac.ap(), aevac.ap(), AF.Ln, bias=1.0)
                            nc.vector.wait_ge(s_s, c["s"])
                            V.tensor_scalar_mul(aevac.ap(), aevac.ap(),
                                                negexpA.ap()[:, 0:1])
                            nc.scalar.wait_ge(s_v, c["v"])
                            S.activation(d_dec.ap()[:, t0:t0 + P1T], aevac.ap(),
                                         AF.Exp)

                    for tt in range(NT1):
                        te_w()
                        nc.tensor.wait_ge(s_s, c["s"])
                        nc.tensor.wait_ge(s_v, c["v"])
                        for b in range(BL):
                            TEN.matmul(pj_ps[1].ap(),
                                       rep8ds.ap()[:, b * 128:(b + 1) * 128],
                                       d_dec.ap()[:, b * T + tt * P1T:
                                                  b * T + (tt + 1) * P1T],
                                       start=(b == 0), stop=(b == BL - 1))
                        nc.vector.wait_ge(s_t, c["t"])
                        V.tensor_copy(d_scan.ap()[:, tt * P1T:(tt + 1) * P1T],
                                      pj_ps[1].ap())
                    barrier()

            if phase_on("p2"):
                with contextlib.ExitStack() as p2:
                    def sb2(name, shape, dt=F32):
                        return p2.enter_context(nc.sbuf_tensor(name, shape, dt))

                    def ps2(name, shape, dt=F32):
                        return p2.enter_context(nc.psum_tensor(name, shape, dt))

                    S_sb = sb2("S_sb", [128, 4 * HV])
                    S_bfb = sb2("S_bfb", [128, 4 * HV], BF16)
                    qkbd = [[sb2(f"qkbd{r}_{n}", [128, 32, Tc], BF16) for n in range(4)]
                            for r in range(2)]
                    knxT = [[sb2(f"knxT{r}_{n}", [16, 128, Tc], BF16) for n in range(4)]
                            for r in range(2)]
                    vz_r = [sb2(f"vz_r{r}", [16, HV, Tc]) for r in range(2)]
                    sq_r = [sb2(f"sq_r{r}", [32, HV, Tc]) for r in range(2)]
                    dstg = [sb2(f"dstg{r}", [16, HV], BF16) for r in range(2)]
                    P_ps = [ps2(f"P_ps{r}", [128, 4 * HV]) for r in range(2)]
                    rq_ps = [ps2(f"rq_ps{r}", [32, HV]) for r in range(2)]

                    V.memset(S_sb.ap(), 0.0)
                    V.memset(S_bfb.ap(), 0.0)
                    V.memset(rq_ps[0].ap(), 0.0)
                    for r in range(2):
                        V.memset(dstg[r].ap(), 0.0)
                    for r in range(2):
                        for n in range(4):
                            V.memset(knxT[r][n].ap().rearrange("p a b -> p (a b)"), 0.0)
                            V.memset(qkbd[r][n].ap().rearrange("p a b -> p (a b)"), 0.0)
                    for e in (nc.sync, nc.tensor, nc.scalar, nc.gpsimd):
                        e.wait_ge(s_v, c["v"])

                    bld_done = {}

                    def emit_builders(ch):
                        r = ch % 2
                        t0 = ch * Tc
                        for n in range(4):
                            tile = n // 2
                            prow0 = (n % 2) * 64
                            for src_sb, moff, toff in ((qn_bf[tile], 16, 0),
                                                       (kn_bf[tile], 0, 1)):
                                for b in range(BL):
                                    for h in range(H):
                                        m = moff + b * 8 + h
                                        dst = bass.AP(
                                            qkbd[r][n],
                                            (b * 64 + h * 8) * 32 * Tc + m * Tc,
                                            [[32 * Tc, 8], [1, Tc]])
                                        srcap = bass.AP(
                                            src_sb,
                                            (prow0 + h * 8) * (BLT + 1) + b * T + t0 + toff,
                                            [[BLT + 1, 8], [1, Tc]])
                                        SY.dma_start(out=dst, in_=srcap).then_inc(s_bld, 16)
                                        c["bld"] += 16
                            for b in range(BL):
                                for h in range(H):
                                    m = b * 8 + h
                                    dst = bass.AP(
                                        knxT[r][n],
                                        m * 128 * Tc + (b * 64 + h * 8) * Tc,
                                        [[128 * Tc, 1], [Tc, 8], [1, Tc]])
                                    srcap = bass.AP(
                                        kn_bf[tile],
                                        (prow0 + h * 8) * (BLT + 1) + b * T + t0,
                                        [[BLT + 1, 8], [1, Tc]])
                                    SY.dma_start(out=dst, in_=srcap).then_inc(s_bld, 16)
                                    c["bld"] += 16
                        SY.dma_start(out=vz_r[r].ap(),
                                     in_=vz_dram[:, :, t0:t0 + Tc]).then_inc(s_vz, 16)
                        c["vz"] += 16
                        bld_done[ch] = c["bld"]

                    emit_builders(0)
                    if NCH > 1:
                        emit_builders(1)

                    te_rq = 0

                    for ch in range(NCH):
                        r = ch % 2
                        for tau in range(Tc):
                            t = ch * Tc + tau
                            pb = t % 2
                            nb = (t + 1) % 2
                            if t > 0:
                                nc.vector.wait_ge(s_t, te_rq)
                            if tau == 0:
                                nc.vector.wait_ge(s_vz, 16 * (ch + 1))
                            V.tensor_sub(dstg[pb].ap(),
                                         vz_r[r].ap()[:, :, tau],
                                         rq_ps[pb].ap()[0:16, :])
                            te_w()
                            nc.tensor.wait_ge(s_v, c["v"])
                            if tau == 0:
                                nc.tensor.wait_ge(s_bld, bld_done[ch])
                            for n in range(4):
                                TEN.matmul(P_ps[pb].ap()[:, n * HV:(n + 1) * HV],
                                           knxT[r][n].ap()[:, :, tau],
                                           dstg[pb].ap(),
                                           start=True, stop=True)
                            nc.vector.wait_ge(s_t, c["t"])
                            V.scalar_tensor_tensor(P_ps[pb].ap(), S_sb.ap(),
                                                   d_scan.ap()[:, t:t + 1],
                                                   P_ps[pb].ap(), Alu.mult, Alu.add)
                            nc.scalar.wait_ge(s_v, c["v"])
                            S.activation(S_sb.ap(), P_ps[pb].ap(), AF.Tanh)
                            nc.gpsimd.wait_ge(s_s, c["s"])
                            G.tensor_copy(S_bfb.ap(), S_sb.ap())
                            te_w()
                            nc.tensor.wait_ge(s_g, c["g"])
                            for n in range(4):
                                TEN.matmul(rq_ps[nb].ap(),
                                           qkbd[r][n].ap()[:, :, tau],
                                           S_bfb.ap()[:, n * HV:(n + 1) * HV],
                                           start=(n == 0), stop=(n == 3))
                            te_rq = c["t"]
                            nc.vector.wait_ge(s_t, te_rq)
                            if tau == 0:
                                nc.vector.wait_ge(s_sqo, max(0, c["sqo"] - 16))
                            V.tensor_copy(sq_r[r].ap()[:, :, tau], rq_ps[nb].ap())

                        SY.wait_ge(s_v, c["v"])
                        SY.dma_start(out=sq_dram[:, :, ch * Tc:(ch + 1) * Tc],
                                     in_=sq_r[r].ap()[16:32, :, :]).then_inc(s_sqo, 16)
                        c["sqo"] += 16
                        if ch + 2 < NCH:
                            SY.wait_ge(s_t, c["t"])
                            emit_builders(ch + 2)

                    barrier()

            if phase_on("p3"):
                with contextlib.ExitStack() as p3:
                    def sb3(name, shape, dt=F32):
                        return p3.enter_context(nc.sbuf_tensor(name, shape, dt))

                    def ps3(name, shape, dt=F32):
                        return p3.enter_context(nc.psum_tensor(name, shape, dt))

                    wo_sb = sb3("wo_sb", [128, D, 4], BF16)
                    G.dma_start(out=wo_sb.ap(),
                                in_=woT.rearrange("(c p) m -> p c m", p=128)).then_inc(s_gld, 16)
                    c["gld"] += 16
                    nc.tensor.wait_ge(s_gld, c["gld"])

                    sqf = [sb3(f"sqf{i}", [128, TTILE]) for i in range(4)]
                    gsf = [sb3(f"gsf{i}", [128, TTILE]) for i in range(4)]
                    ubuf = [sb3(f"ubuf{i}", [128, TTILE]) for i in range(4)]
                    xt3 = sb3("xt3", [128, 8, TTILE])
                    sg3 = sb3("sg3", [128, TTILE])
                    u2b = sb3("u2b", [128, 4, TTILE], BF16)
                    att_bf = [sb3(f"att_bf{i}", [128, TTILE], BF16) for i in range(4)]
                    rmsv = sb3("rmsv", [H, TTILE])
                    xab = sb3("xab", [128, TTILE])
                    rms_ps = ps3("rms_ps", [H, TTILE])
                    rrep_ps = ps3("rrep_ps", [128, TTILE])
                    y_ps = [ps3(f"y_ps{i}", [128, TTILE]) for i in range(2)]

                    for b in range(BL):
                        for tt in range(NTT):
                            t0 = b * T + tt * TTILE
                            tl = tt * TTILE
                            SY.wait_ge(s_v, c["v"])
                            SY.wait_ge(s_t, c["t"])
                            SY.wait_ge(s_s, c["s"])
                            for i in range(4):
                                SY.dma_start(
                                    out=sqf[i].ap(),
                                    in_=bass.AP(sq_dram.tensor,
                                                (b * 8 + 2 * i) * HV * T + tl,
                                                [[HV * T, 2], [T, HV], [1, TTILE]]),
                                ).then_inc(s_x, 16)
                                c["x"] += 16
                                SY.dma_start(out=gsf[i].ap(),
                                             in_=gate_dram[i * 128:(i + 1) * 128,
                                                           t0:t0 + TTILE]).then_inc(s_x, 16)
                                c["x"] += 16
                            for dc in range(8):
                                SY.dma_start(
                                    out=xt3.ap()[:, dc, :],
                                    in_=xT[dc * 128:(dc + 1) * 128, t0:t0 + TTILE],
                                ).then_inc(s_x, 16)
                                c["x"] += 16
                            nc.scalar.wait_ge(s_x, c["x"])
                            nc.vector.wait_ge(s_x, c["x"])
                            for i in range(4):
                                if i >= 1:
                                    nc.scalar.wait_ge(s_v, c["v"])
                                S.activation(sg3.ap(), sqf[i].ap(), AF.Sigmoid)
                                nc.vector.wait_ge(s_s, c["s"])
                                V.tensor_mul(ubuf[i].ap(), sqf[i].ap(), sqf[i].ap())
                                V.tensor_mul(ubuf[i].ap(), ubuf[i].ap(), sg3.ap())
                                V.tensor_mul(u2b.ap()[:, i, :], ubuf[i].ap(),
                                             ubuf[i].ap())
                            te_w()
                            nc.tensor.wait_ge(s_v, c["v"])
                            for i in range(4):
                                TEN.matmul(rms_ps.ap(), hv8s.ap()[:, i * H:(i + 1) * H],
                                           u2b.ap()[:, i, :],
                                           start=(i == 0), stop=(i == 3))
                            nc.scalar.wait_ge(s_t, c["t"])
                            S.activation(rmsv.ap(), rms_ps.ap(), AF.Sqrt,
                                         scale=rms_sc_h.ap(), bias=epsrms_h.ap())
                            nc.vector.wait_ge(s_s, c["s"])
                            V.reciprocal(rmsv.ap(), rmsv.ap())
                            for i in range(4):
                                te_w()
                                nc.tensor.wait_ge(s_v, c["v"])
                                TEN.matmul(rrep_ps.ap(),
                                           repfms.ap()[:, i * 128:(i + 1) * 128],
                                           rmsv.ap(), start=True, stop=True)
                                nc.vector.wait_ge(s_t, c["t"])
                                V.scalar_tensor_tensor(ubuf[i].ap(), ubuf[i].ap(),
                                                       onorm_pp.ap(), rrep_ps.ap(),
                                                       Alu.mult, Alu.mult)
                                V.tensor_mul(att_bf[i].ap(), ubuf[i].ap(),
                                             gsf[i].ap())
                            for j in range(8):
                                yp = y_ps[j % 2]
                                te_w()
                                nc.tensor.wait_ge(s_v, c["v"])
                                for i in range(4):
                                    TEN.matmul(yp.ap(),
                                               wo_sb.ap()[:, j * 128:(j + 1) * 128, i],
                                               att_bf[i].ap(),
                                               start=(i == 0), stop=(i == 3))
                                nc.vector.wait_ge(s_t, c["t"])
                                nc.vector.wait_ge(s_io, c["io"])
                                V.tensor_add(xab.ap(), yp.ap(), xt3.ap()[:, j, :])
                                SY.wait_ge(s_v, c["v"])
                                SY.dma_start(out=xa_dram[j * 128:(j + 1) * 128,
                                                         t0:t0 + TTILE],
                                             in_=xab.ap()).then_inc(s_io, 16)
                                c["io"] += 16
                    barrier()

            if phase_on("p4a"):
                with contextlib.ExitStack() as p4a:
                    def sb4(name, shape, dt=F32):
                        return p4a.enter_context(nc.sbuf_tensor(name, shape, dt))

                    def ps4(name, shape, dt=F32):
                        return p4a.enter_context(nc.psum_tensor(name, shape, dt))

                    xt4 = sb4("xt4", [128, 8, TTILE])
                    xt4_bf = sb4("xt4_bf", [128, 8, TTILE], BF16)
                    x2b4 = sb4("x2b4", [128, 8, TTILE], BF16)
                    ln4 = sb4("ln4", [128, TTILE])
                    mu4 = sb4("mu4", [1, TTILE])
                    musq4 = sb4("musq4", [1, TTILE])
                    var4 = sb4("var4", [1, TTILE])
                    rstd4 = sb4("rstd4", [1, TTILE])
                    nmurstd4 = sb4("nmurstd4", [1, TTILE])
                    part4 = sb4("part4", [128, 1])
                    gctx = sb4("gctx", [128, 8, BL])
                    sx_ps4 = ps4("sx_ps4", [1, TTILE])
                    sx2_ps4 = ps4("sx2_ps4", [1, TTILE])
                    rep_ps4 = ps4("rep_ps4", [128, 2 * TTILE])
                    g_ps = ps4("g_ps", [128, BL])
                    g1_ps = ps4("g1_ps", [1, BL])

                    V.memset(gctx.ap().rearrange("p a b -> p (a b)"), 0.0)

                    for b in range(BL):
                        for tt in range(NTT):
                            t0 = b * T + tt * TTILE
                            SY.wait_ge(s_v, c["v"])
                            for dc in range(8):
                                SY.dma_start(
                                    out=xt4.ap()[:, dc, :],
                                    in_=xa_dram[dc * 128:(dc + 1) * 128,
                                                t0:t0 + TTILE],
                                ).then_inc(s_x, 16)
                                c["x"] += 16
                            nc.vector.wait_ge(s_x, c["x"])
                            nc.vector.wait_ge(s_t, c["t"])
                            V.tensor_copy(xt4_bf.ap().rearrange("p a t -> p (a t)"),
                                          xt4.ap().rearrange("p a t -> p (a t)"))
                            V.tensor_mul(x2b4.ap().rearrange("p a t -> p (a t)"),
                                         xt4.ap().rearrange("p a t -> p (a t)"),
                                         xt4.ap().rearrange("p a t -> p (a t)"))
                            te_w()
                            nc.tensor.wait_ge(s_v, c["v"])
                            for dc in range(8):
                                TEN.matmul(sx_ps4.ap(), ones128.ap(),
                                           xt4_bf.ap()[:, dc, :],
                                           start=(dc == 0), stop=(dc == 7))
                                TEN.matmul(sx2_ps4.ap(), ones128.ap(),
                                           x2b4.ap()[:, dc, :],
                                           start=(dc == 0), stop=(dc == 7))
                            nc.vector.wait_ge(s_t, c["t"])
                            V.tensor_scalar_mul(mu4.ap(), sx_ps4.ap(), 1.0 / D)
                            V.tensor_mul(musq4.ap(), mu4.ap(), mu4.ap())
                            V.scalar_tensor_tensor(var4.ap(), sx2_ps4.ap(), 1.0 / D,
                                                   musq4.ap(), Alu.mult, Alu.subtract)
                            nc.scalar.wait_ge(s_v, c["v"])
                            S.activation(rstd4.ap(), var4.ap(), AF.Sqrt,
                                         bias=epsln_1.ap())
                            nc.vector.wait_ge(s_s, c["s"])
                            V.reciprocal(rstd4.ap(), rstd4.ap())
                            V.scalar_tensor_tensor(nmurstd4.ap(), mu4.ap(), -1.0,
                                                   rstd4.ap(), Alu.mult, Alu.mult)
                            te_w()
                            nc.tensor.wait_ge(s_v, c["v"])
                            TEN.matmul(rep_ps4.ap()[:, 0:TTILE], ones1.ap(),
                                       rstd4.ap(), start=True, stop=True)
                            TEN.matmul(rep_ps4.ap()[:, TTILE:], ones1.ap(),
                                       nmurstd4.ap(), start=True, stop=True)
                            nc.vector.wait_ge(s_t, c["t"])
                            for dc in range(8):
                                nc.vector.wait_ge(s_io, c["io"])
                                V.tensor_mul(ln4.ap(), xt4.ap()[:, dc, :],
                                             rep_ps4.ap()[:, 0:TTILE])
                                V.scalar_tensor_tensor(ln4.ap(), ln4.ap(), 1.0,
                                                       rep_ps4.ap()[:, TTILE:],
                                                       Alu.mult, Alu.add,
                                                       accum_out=part4.ap())
                                V.tensor_add(gctx.ap()[:, dc, b:b + 1],
                                             gctx.ap()[:, dc, b:b + 1], part4.ap())
                                SY.wait_ge(s_v, c["v"])
                                SY.dma_start(out=ln2_dram[dc * 128:(dc + 1) * 128,
                                                          t0:t0 + TTILE],
                                             in_=ln4.ap()).then_inc(s_io, 16)
                                c["io"] += 16
                    V.tensor_scalar_mul(gctx.ap().rearrange("p a b -> p (a b)"),
                                        gctx.ap().rearrange("p a b -> p (a b)"),
                                        1.0 / T)

                    gw1_sb = sb4("gw1_sb", [128, 512, 8])
                    gw2_sb = sb4("gw2_sb", [128, 256, 4])
                    gw3_sb = sb4("gw3_sb", [128, 1, 2])
                    SY.dma_start(out=gw1_sb.ap(),
                                 in_=gw1.rearrange("(c p) m -> p c m", p=128)).then_inc(s_ld, 16)
                    c["ld"] += 16
                    SY.dma_start(out=gw2_sb.ap(),
                                 in_=gw2.rearrange("(c p) m -> p c m", p=128)).then_inc(s_ld, 16)
                    c["ld"] += 16
                    SY.dma_start(out=gw3_sb.ap(),
                                 in_=gw3.rearrange("(c p) m -> p c m", p=128),
                                 allow_slow_non_contiguous=True).then_inc(s_ld, 16)
                    c["ld"] += 16
                    nc.tensor.wait_ge(s_ld, c["ld"])
                    h1g = sb4("h1g", [128, 4, BL])
                    h2g = sb4("h2g", [128, 2, BL])
                    for m in range(4):
                        te_w()
                        nc.tensor.wait_ge(s_v, c["v"])
                        nc.tensor.wait_ge(s_s, c["s"])
                        for dc in range(8):
                            TEN.matmul(g_ps.ap(),
                                       gw1_sb.ap()[:, m * 128:(m + 1) * 128, dc],
                                       gctx.ap()[:, dc, :],
                                       start=(dc == 0), stop=(dc == 7))
                        nc.scalar.wait_ge(s_t, c["t"])
                        S.activation(h1g.ap()[:, m, :], g_ps.ap(), AF.Gelu,
                                     bias=gb1_sb.ap()[:, m:m + 1])
                    for m in range(2):
                        te_w()
                        nc.tensor.wait_ge(s_s, c["s"])
                        for dc in range(4):
                            TEN.matmul(g_ps.ap(),
                                       gw2_sb.ap()[:, m * 128:(m + 1) * 128, dc],
                                       h1g.ap()[:, dc, :],
                                       start=(dc == 0), stop=(dc == 3))
                        nc.scalar.wait_ge(s_t, c["t"])
                        S.activation(h2g.ap()[:, m, :], g_ps.ap(), AF.Gelu,
                                     bias=gb2_sb.ap()[:, m:m + 1])
                    te_w()
                    nc.tensor.wait_ge(s_s, c["s"])
                    for dc in range(2):
                        TEN.matmul(g1_ps.ap(), gw3_sb.ap()[:, :, dc],
                                   h2g.ap()[:, dc, :],
                                   start=(dc == 0), stop=(dc == 1))
                    nc.scalar.wait_ge(s_t, c["t"])
                    S.activation(gf_A.ap(), g1_ps.ap(), AF.Sigmoid,
                                 bias=gb3_sb.ap())
                    nc.vector.wait_ge(s_s, c["s"])
                    nc.vector.wait_ge(s_ld, c["ld"])
                    V.tensor_scalar(gf_A.ap(), gf_A.ap(), ar_sb.ap(), 1.4,
                                    Alu.mult, Alu.mult)
                    V.tensor_scalar(gf_B.ap(), ar_sb.ap(), -1.0, 0.1,
                                    Alu.mult, Alu.add)
                    V.tensor_scalar_add(gf_A.ap(), gf_A.ap(), gf_B.ap())
                    V.tensor_scalar_mul(gf_B.ap(), ar_sb.ap(), 0.6)
                    barrier()

            if phase_on("p4b"):
                with contextlib.ExitStack() as p4b:
                    def sb5(name, shape, dt=F32):
                        return p4b.enter_context(nc.sbuf_tensor(name, shape, dt))

                    def ps5(name, shape, dt=F32):
                        return p4b.enter_context(nc.psum_tensor(name, shape, dt))

                    cw1_sb = sb5("cw1_sb", [128, D, 8], BF16)
                    cw2_sb = sb5("cw2_sb", [128, 512, 8], BF16)
                    cw3_sb = sb5("cw3_sb", [128, 256, 4], BF16)
                    cw4_sb = sb5("cw4_sb", [128, 1, 2], BF16)
                    for wsb, wdr in ((cw1_sb, cw1), (cw2_sb, cw2), (cw3_sb, cw3),
                                     (cw4_sb, cw4)):
                        G.dma_start(out=wsb.ap(),
                                    in_=wdr.rearrange("(c p) m -> p c m", p=128),
                                    allow_slow_non_contiguous=(wdr is cw4)).then_inc(s_gld, 16)
                        c["gld"] += 16
                    nc.tensor.wait_ge(s_gld, c["gld"])

                    outb = sb5("outb", [128, 8, TTILE])
                    outb_bf = sb5("outb_bf", [128, 8, TTILE], BF16)
                    h1b = sb5("h1b", [128, 8, TTILE], BF16)
                    h2b = sb5("h2b", [128, 4, TTILE], BF16)
                    h3b = sb5("h3b", [128, 2, TTILE], BF16)
                    tmpb = sb5("tmpb", [128, 8, TTILE])
                    cpl = sb5("cpl", [1, TTILE])
                    xt5 = sb5("xt5", [128, 8, TTILE])
                    h_ps = [ps5(f"h_ps{i}", [128, TTILE]) for i in range(2)]
                    c1_ps = ps5("c1_ps", [1, TTILE])
                    crep_ps = ps5("crep_ps", [128, TTILE])

                    tw_src = [4, 5, 6, 7, 0, 1, 2, 3]
                    tw_sgn = [1, 1, -1, -1, -1, -1, 1, 1]

                    def mlp_layer(win, bsb, src, src_sem, src_cnt, dst, n_m, n_dc):
                        for m in range(n_m):
                            hp = h_ps[m % 2]
                            te_w()
                            nc.tensor.wait_ge(src_sem, src_cnt)
                            nc.tensor.wait_ge(s_s, c["s"])
                            for dc in range(n_dc):
                                TEN.matmul(hp.ap(),
                                           win.ap()[:, m * 128:(m + 1) * 128, dc],
                                           src.ap()[:, dc, :],
                                           start=(dc == 0), stop=(dc == n_dc - 1))
                            nc.scalar.wait_ge(s_t, c["t"])
                            S.activation(dst.ap()[:, m, :], hp.ap(), AF.Gelu,
                                         bias=bsb.ap()[:, m:m + 1])

                    for b in range(BL):
                        for tt in range(NTT):
                            t0 = b * T + tt * TTILE
                            SY.wait_ge(s_v, c["v"])
                            SY.wait_ge(s_g, c["g"])
                            for dc in range(8):
                                SY.dma_start(
                                    out=outb.ap()[:, dc, :],
                                    in_=ln2_dram[dc * 128:(dc + 1) * 128,
                                                 t0:t0 + TTILE],
                                ).then_inc(s_x, 16)
                                c["x"] += 16
                            for cyc in range(3):
                                nc.gpsimd.wait_ge(s_x, c["x"])
                                nc.gpsimd.wait_ge(s_v, c["v"])
                                G.tensor_copy(outb_bf.ap().rearrange("p a t -> p (a t)"),
                                              outb.ap().rearrange("p a t -> p (a t)"))
                                mlp_layer(cw1_sb, cb1_sb, outb_bf, s_g, c["g"],
                                          h1b, 8, 8)
                                mlp_layer(cw2_sb, cb2_sb, h1b, s_s, c["s"],
                                          h2b, 4, 8)
                                mlp_layer(cw3_sb, cb3_sb, h2b, s_s, c["s"],
                                          h3b, 2, 4)
                                te_w()
                                nc.tensor.wait_ge(s_s, c["s"])
                                for dc in range(2):
                                    TEN.matmul(c1_ps.ap(), cw4_sb.ap()[:, :, dc],
                                               h3b.ap()[:, dc, :],
                                               start=(dc == 0), stop=(dc == 1))
                                nc.scalar.wait_ge(s_t, c["t"])
                                S.activation(cpl.ap(), c1_ps.ap(), AF.Sigmoid,
                                             bias=cb4_sb.ap())
                                nc.vector.wait_ge(s_s, c["s"])
                                V.tensor_scalar(cpl.ap(), cpl.ap(), gf_B.ap(),
                                                gf_A.ap()[:, b:b + 1],
                                                Alu.mult, Alu.add)
                                te_w()
                                nc.tensor.wait_ge(s_v, c["v"])
                                TEN.matmul(crep_ps.ap(), ones1.ap(),
                                           cpl.ap(), start=True, stop=True)
                                nc.vector.wait_ge(s_t, c["t"])
                                if cyc == 0:
                                    nc.vector.wait_ge(s_x, c["x"])
                                    nc.vector.wait_ge(s_io, c["io"])
                                for i in range(8):
                                    V.tensor_mul(tmpb.ap()[:, i, :],
                                                 outb.ap()[:, tw_src[i], :],
                                                 crep_ps.ap())
                                for i in range(8):
                                    V.tensor_tensor(outb.ap()[:, i, :],
                                                    outb.ap()[:, i, :],
                                                    tmpb.ap()[:, i, :],
                                                    Alu.add if tw_sgn[i] > 0
                                                    else Alu.subtract)
                            SY.wait_ge(s_v, c["v"])
                            for dc in range(8):
                                SY.dma_start(
                                    out=xt5.ap()[:, dc, :],
                                    in_=xa_dram[dc * 128:(dc + 1) * 128,
                                                t0:t0 + TTILE],
                                ).then_inc(s_x, 16)
                                c["x"] += 16
                            nc.vector.wait_ge(s_x, c["x"])
                            for j in range(8):
                                V.tensor_add(tmpb.ap()[:, j, :], outb.ap()[:, j, :],
                                             xt5.ap()[:, j, :])
                            SY.wait_ge(s_v, c["v"])
                            for dc in range(8):
                                SY.dma_start(
                                    out=yT[dc * 128:(dc + 1) * 128, t0:t0 + TTILE],
                                    in_=tmpb.ap()[:, dc, :],
                                ).then_inc(s_io, 16)
                                c["io"] += 16

                    SY.wait_ge(s_io, c["io"])
            barrier()
            if stop_phase != "full":
                with nc.sbuf_tensor("dbg_cp", [128, BLT], F32) as dbg_cp:
                    for j in range(8):
                        SY.wait_ge(s_io, c["io"])
                        SY.dma_start(out=dbg_cp.ap(),
                                     in_=xT[j * 128:(j + 1) * 128, :]).then_inc(s_x, 16)
                        c["x"] += 16
                        SY.wait_ge(s_x, c["x"])
                        SY.dma_start(out=yT[j * 128:(j + 1) * 128, :],
                                     in_=dbg_cp.ap()).then_inc(s_io, 16)
                        c["io"] += 16
                    SY.wait_ge(s_io, c["io"])
                barrier()

        blk_box["cm"].__exit__(None, None, None)
        st.close()
        return nc


def _hsum_pat():
    m = np.zeros((128, H), np.float32)
    for r in range(128):
        m[r, (r % 64) // 8] = 1.0
    return m


def _rep8h_pat():
    m = np.zeros((H, 128), np.float32)
    for cc in range(128):
        m[(cc % 64) // 8, cc] = 1.0
    return m


def _hv8_pat():
    m = np.zeros((128, 4 * H), np.float32)
    for i in range(4):
        for r in range(128):
            m[r, i * H + 2 * i + r // 64] = 1.0
    return m


def _repfm_pat():
    m = np.zeros((H, 4 * 128), np.float32)
    for i in range(4):
        for cc in range(128):
            m[2 * i + cc // 64, i * 128 + cc] = 1.0
    return m


def _rep8d_pat():
    m = np.zeros((H, BL * 128), np.float32)
    for b in range(BL):
        for p in range(128):
            if p // 64 == b:
                m[(p % 64) // 8, b * 128 + p] = 1.0
    return m


def _wtile(wT, n_m):
    d_in, m_ = wT.shape
    return _r(wT.reshape(d_in // 128, 128, m_).transpose(1, 2, 0).reshape(128, -1))


def stage_inputs(inputs, core):
    f32 = np.float32
    bf16 = ml_dtypes.bfloat16

    x = np.asarray(inputs["L0_attn_x"], f32)
    T = x.shape[1]
    BLT = BL * T
    xs = x[2 * core:2 * core + 2]
    xT = _r(xs.transpose(2, 0, 1).reshape(D, BLT))

    perm = qk_perm()
    wq = np.asarray(inputs["L0_attn_Wq"], f32)[perm]
    wk = np.asarray(inputs["L0_attn_Wk"], f32)[perm]
    wv = np.asarray(inputs["L0_attn_Wv"], f32)
    wa = np.asarray(inputs["L0_attn_Wa"], f32)
    wg = np.asarray(inputs["L0_attn_Wg"], f32)
    wo = np.asarray(inputs["L0_attn_Wo"], f32)

    qc = np.asarray(inputs["L0_attn_q_conv_w"], f32)[:, 0, :][perm]
    kc = np.asarray(inputs["L0_attn_k_conv_w"], f32)[:, 0, :][perm]
    vc = np.asarray(inputs["L0_attn_v_conv_w"], f32)[:, 0, :]

    m = {
        "xT": xT,
        "wqT": _wtile(wq.T.astype(bf16), KD),
        "wkT": _wtile(wk.T.astype(bf16), KD),
        "wvT": _wtile(wv.T.astype(bf16), VD),
        "waT": _wtile(wa.T.astype(bf16), H),
        "wgT": _wtile(wg.T.astype(bf16), VD),
        "woT": _wtile(wo.T.astype(bf16), D),
        "qcw": _r(qc.reshape(2, 128, DC).transpose(1, 0, 2).reshape(128, 2 * DC)),
        "kcw": _r(kc.reshape(2, 128, DC).transpose(1, 0, 2).reshape(128, 2 * DC)),
        "vcw": _r(vc.reshape(4, 128, DC).transpose(1, 0, 2).reshape(128, 4 * DC)),
        "alog": _r(np.asarray(inputs["L0_attn_A_log"], f32).reshape(H, 1)),
        "dtb": _r(np.asarray(inputs["L0_attn_dt_bias"], f32).reshape(H, 1)),
        "onorm": _r(np.tile(np.asarray(inputs["L0_attn_o_norm_w"], f32), 2)
                    .reshape(128, 1)),
        "cw1": _wtile(np.asarray(inputs["L0_ffn_cn_w1"], f32).astype(bf16), D),
        "cw2": _wtile(np.asarray(inputs["L0_ffn_cn_w2"], f32).astype(bf16), 512),
        "cw3": _wtile(np.asarray(inputs["L0_ffn_cn_w3"], f32).astype(bf16), 256),
        "cw4": _wtile(np.asarray(inputs["L0_ffn_cn_w4"], f32).astype(bf16), 1),
        "cb1": _r(np.asarray(inputs["L0_ffn_cn_b1"], f32).reshape(8, 128).T),
        "cb2": _r(np.asarray(inputs["L0_ffn_cn_b2"], f32).reshape(4, 128).T),
        "cb3": _r(np.asarray(inputs["L0_ffn_cn_b3"], f32).reshape(2, 128).T),
        "cb4": _r(np.asarray(inputs["L0_ffn_cn_b4"], f32).reshape(1, 1)),
        "gw1": _wtile(np.asarray(inputs["L0_ffn_gc_w1"], f32), 512),
        "gw2": _wtile(np.asarray(inputs["L0_ffn_gc_w2"], f32), 256),
        "gw3": _wtile(np.asarray(inputs["L0_ffn_gc_w3"], f32), 1),
        "gb1": _r(np.asarray(inputs["L0_ffn_gc_b1"], f32).reshape(4, 128).T),
        "gb2": _r(np.asarray(inputs["L0_ffn_gc_b2"], f32).reshape(2, 128).T),
        "gb3": _r(np.asarray(inputs["L0_ffn_gc_b3"], f32).reshape(1, 1)),
        "arng": _r(np.asarray(inputs["L0_ffn_adaptive_range"], f32).reshape(1, 1)),
        "p_ones128": np.ones((128, 1), bf16),
        "p_ones1": np.ones((1, 128), f32),
        "p_hsum": _hsum_pat().astype(bf16),
        "p_rep8h": _rep8h_pat(),
        "p_hv8": _hv8_pat().astype(bf16),
        "p_repfm": _repfm_pat(),
        "p_rep8d": _rep8d_pat(),
    }
    return m


def kernel(**inputs):
    prog = Prog(T=2048)
    nc = prog.build()
    in_maps = [stage_inputs(inputs, core) for core in range(NCORES)]
    res = run_bass_kernel_spmd(nc, in_maps, core_ids=list(range(NCORES)))
    out = np.empty((16, 2048, D), np.float32)
    for core in range(NCORES):
        yT = np.asarray(res.results[core]["yT"])
        out[2 * core:2 * core + 2] = yT.reshape(D, 2, 2048).transpose(1, 2, 0)
    return out
